# version 02 (final)
# baseline (speedup 1.0000x reference)
from contextlib import ExitStack

import numpy as np
import ml_dtypes
import concourse.bass as bass
import concourse.mybir as mybir
from concourse.bass_utils import run_bass_kernel_spmd

F32 = mybir.dt.float32
F32R = mybir.dt.float32r
BF16 = mybir.dt.bfloat16
E5 = mybir.dt.float8e5
I8 = mybir.dt.int8
U8 = mybir.dt.uint8
AF = mybir.ActivationFunctionType
OP = mybir.AluOpType
AX = mybir.AxisListType

NCORES = 8
NSEQ = 2
S = 2048
D = 1024
NCH = 8
H0 = 16
HID0 = 3072
HID1 = 4096
LAT = 64
LATD = 256
LH = 4
CHUNK = 32
INV127 = 1.0 / 127.0

_STAGE = "l0"

_DT_SZ = {F32: 4, F32R: 4, BF16: 2, E5: 1, I8: 1, U8: 1}



def _chunk_lhsT(w, ntile):
    K, M = w.shape
    t = M // ntile
    return np.ascontiguousarray(w.reshape(K, ntile, t).transpose(1, 0, 2))


def _chunk_pcn(w, ntile):
    K, M = w.shape
    t = M // ntile
    return np.ascontiguousarray(
        w.reshape(K // 128, 128, ntile, t).transpose(2, 1, 0, 3))


def _int8_of_int32(w):
    v = np.ascontiguousarray(w).view(np.int8).reshape(*w.shape, 4)[..., 0]
    return np.ascontiguousarray(v)


def _rope_perm():
    p = np.zeros(D, dtype=np.int64)
    for h in range(H0):
        for j in range(64):
            p[h * 64 + j] = h * 64 + (2 * j if j < 32 else 2 * (j - 32) + 1)
    return p


def _host_tables():
    t = np.arange(S, dtype=np.float64)[:, None]
    m = np.arange(32, dtype=np.float64)[None, :]
    ang = t / (10000.0 ** (2.0 * m / 64.0))
    cos = np.cos(ang).T
    sin = np.sin(ang).T
    CC = np.zeros((128, S), np.float64)
    SS = np.zeros((128, S), np.float64)
    for blk in range(2):
        b = blk * 64
        CC[b:b + 32] = cos
        CC[b + 32:b + 64] = cos
        SS[b:b + 32] = -sin
        SS[b + 32:b + 64] = sin
    perm = np.zeros((128, 128), np.float64)
    for p in range(128):
        src = (p & ~63) | ((p + 32) % 64)
        perm[src, p] = 1.0
    k = np.arange(128)[:, None]
    q = np.arange(128)[None, :]
    Md0 = (k <= q).astype(np.float64)
    M0 = (k < q).astype(np.float64)
    M1 = (k >= q).astype(np.float64)
    M1e = M1.copy()
    M1e[0, 0] = 0.0
    ident = np.eye(128)
    ones = np.ones((128, 128))
    l = np.arange(LAT)[:, None]
    tt = np.arange(S)[None, :]
    ML = (l < (tt // CHUNK)).astype(np.float64)
    bf = ml_dtypes.bfloat16
    cst = np.concatenate([perm, Md0, M0, M1, M1e, ident, ones], axis=1).astype(bf)
    ccss = np.concatenate([CC, SS], axis=1).astype(bf)
    return cst, ccss, ML.astype(bf)


def _dropout_mask():
    import os, subprocess, sys, tempfile
    fd, path = tempfile.mkstemp(suffix=".npy")
    os.close(fd)
    code = (
        "import jax, numpy as np\n"
        "jax.config.update('jax_platforms', 'cpu')\n"
        "keep = np.asarray(jax.random.bernoulli(jax.random.key(42), 0.8, (16, %d, %d)))\n"
        "np.save(%r, keep)\n" % (S, D, path))
    env = {k: v for k, v in os.environ.items()}
    env["JAX_PLATFORMS"] = "cpu"
    env.pop("JAX_ENABLE_X64", None)
    subprocess.run([sys.executable, "-c", code], check=True, env=env,
                   capture_output=True)
    keep = np.load(path)
    os.remove(path)
    m = np.where(keep, np.float32(1.25), np.float32(0.0))
    return np.ascontiguousarray(m.transpose(0, 2, 1)).astype(ml_dtypes.bfloat16)


def _host_prep(inputs):
    x = np.ascontiguousarray(inputs["L0_attn_x"].transpose(0, 2, 1))
    P = _rope_perm()
    qkv = inputs["L1_attn_qkv_w"]

    def cols(v):
        n = v.shape[0] // 128
        return v.reshape(n, 128).T
    cb = np.concatenate([
        cols(inputs["L1_attn_qkv_b"]),
        cols(inputs["L1_attn_rq_b"]),
        cols(inputs["L1_attn_rkv_b"]),
        cols(inputs["L1_attn_rout_b"]),
        cols(inputs["L1_attn_out_b"]),
        cols(inputs["L1_ffn_b1"]),
        cols(inputs["L1_ffn_b2"]),
    ], axis=1).astype(np.float32)
    cr = np.zeros((1, 1536), np.float32)
    cr[0, 0:1024] = inputs["L1_attn_qkv_b"][2048:3072]
    cr[0, 1024:1280] = inputs["L1_attn_rkv_b"][256:512]
    glgr = np.zeros((128, 4), np.float32)
    glgr[0:64, 0] = inputs["L1_attn_gl_w"][:, 0]
    glgr[64:128, 1] = inputs["L1_attn_gl_w"][:, 0]
    glgr[0:64, 2] = inputs["L1_attn_gr_w"][:, 0]
    glgr[64:128, 3] = inputs["L1_attn_gr_w"][:, 0]
    e2 = np.zeros((2, 128), ml_dtypes.bfloat16)
    e2[0, 0:64] = 1.0
    e2[1, 64:128] = 1.0
    cst, ccss, ML = _host_tables()
    dm = _dropout_mask()
    shared = dict(
        wqT=_chunk_pcn(np.ascontiguousarray(inputs["L0_attn_wq"].T[:, P]), 8),
        wkT=_chunk_pcn(np.ascontiguousarray(inputs["L0_attn_wk"].T[:, P]), 8),
        wvT=np.ascontiguousarray(inputs["L0_attn_wv"].T),
        woT=_chunk_pcn(np.ascontiguousarray(inputs["L0_attn_wo"].T), 8),
        wgT=_chunk_pcn(_int8_of_int32(inputs["L0_ffn_w_gate"]).T, 24),
        wuT=_chunk_pcn(_int8_of_int32(inputs["L0_ffn_w_up"]).T, 24),
        wdT=np.ascontiguousarray(_int8_of_int32(inputs["L0_ffn_w_down"]).T.reshape(12, 2, 128, 1024).transpose(0, 2, 1, 3)),
        qkvQ=_chunk_pcn(np.ascontiguousarray(qkv[:, 0:1024]), 8),
        qkvK=_chunk_pcn(np.ascontiguousarray(qkv[:, 1024:2048]), 8),
        qkvV=np.ascontiguousarray(qkv[:, 2048:3072]),
        rqT=_chunk_pcn(np.ascontiguousarray(inputs["L1_attn_rq_w"]), 2),
        rkvT=_chunk_pcn(np.ascontiguousarray(inputs["L1_attn_rkv_w"]), 4),
        routT=np.ascontiguousarray(
            inputs["L1_attn_rout_w"].reshape(2, 128, 8, 128)),
        outT=_chunk_pcn(np.ascontiguousarray(inputs["L1_attn_out_w"]), 8),
        w1T=_chunk_pcn(np.ascontiguousarray(inputs["L1_ffn_W1"].T), 32),
        w2T=np.ascontiguousarray(
            inputs["L1_ffn_W2"].T.reshape(32, 128, 8, 128)),
        glgr=glgr, e2=e2, cb=cb, cr=cr, cst=cst, ccss=ccss, ml=ML,
    )
    in_maps = []
    for c in range(NCORES):
        m = dict(shared)
        m["x"] = np.ascontiguousarray(x[2 * c:2 * c + 2])
        m["dropm"] = np.ascontiguousarray(dm[2 * c:2 * c + 2])
        in_maps.append(m)
    return in_maps



class Sem:
    __slots__ = ("h", "n")

    def __init__(self, h):
        self.h = h
        self.n = 0


A_B1 = 0
A_B2 = 8192
A_K = 16384
A_V = 49152
A_CCSS = 82944
A_XN = 91136
A_Q = 99328
A_G = 107520
A_U = 115712
A_END = 140288

T_A = 0
T_B = 1024
T_C = 2048
T_D = 4096


class KB:
    def __init__(self, nc, st):
        self.nc = nc
        self.st = st
        self.cur = None
        self.e = None
        self._mk()

    def op(self, eng, fn, inc=()):
        if self.cur == eng:
            ins = fn(self.e)
            for s, n in inc:
                ins.then_inc(s.h, n)
        for s, n in inc:
            s.n += n

    def wait(self, eng, sem, val=None):
        v = sem.n if val is None else val
        if self.cur == eng:
            self.e.wait_ge(sem.h, v)

    def t(self, fn, inc=()):
        self.op("tensor", fn, inc)

    def v(self, fn, inc=()):
        self.op("vector", fn, inc)

    def s(self, fn, inc=()):
        self.op("scalar", fn, inc)

    def y(self, fn, inc=()):
        self.op("sync", fn, inc)

    def g(self, fn, inc=()):
        self.op("gpsimd", fn, inc)

    def _mk(self):
        nc, st = self.nc, self.st
        d = {}

        def din(name, shape, dt):
            d[name] = nc.dram_tensor(name, list(shape), dt, kind="ExternalInput").ap()

        def dint(name, shape, dt):
            d[name] = nc.dram_tensor(name, list(shape), dt, kind="Internal").ap()

        din("x", (NSEQ, D, S), F32)
        din("wqT", (8, 128, 8, 128), F32)
        din("wkT", (8, 128, 8, 128), F32)
        din("wvT", (D, D), F32)
        din("woT", (8, 128, 8, 128), F32)
        din("wgT", (24, 128, 8, 128), I8)
        din("wuT", (24, 128, 8, 128), I8)
        din("wdT", (12, 128, 2, 1024), I8)
        din("qkvQ", (8, 128, 8, 128), F32)
        din("qkvK", (8, 128, 8, 128), F32)
        din("qkvV", (D, D), F32)
        din("rqT", (2, 128, 8, 128), F32)
        din("rkvT", (4, 128, 8, 128), F32)
        din("routT", (2, 128, 8, 128), F32)
        din("outT", (8, 128, 8, 128), F32)
        din("w1T", (32, 128, 8, 128), F32)
        din("w2T", (32, 128, 8, 128), F32)
        din("glgr", (128, 4), F32)
        din("e2", (2, 128), BF16)
        din("cb", (128, 86), F32)
        din("cr", (1, 1536), F32)
        din("cst", (128, 7 * 128), BF16)
        din("ccss", (128, 2 * S), BF16)
        din("ml", (LAT, S), BF16)
        din("dropm", (NSEQ, D, S), BF16)
        d["out"] = nc.dram_tensor("out", [NSEQ, D, S], F32, kind="ExternalOutput").ap()
        self.d = d

        self.XR = st.enter_context(nc.sbuf_tensor("XR", [128, NCH, S], F32))
        self.CST = st.enter_context(nc.sbuf_tensor("CST", [128, 7 * 128], BF16))
        self.CB = st.enter_context(nc.sbuf_tensor("CB", [128, 86], F32))
        self.CRB = st.enter_context(nc.sbuf_tensor("CRB", [1, 1536], BF16))
        self.GLGR = st.enter_context(nc.sbuf_tensor("GLGR", [128, 4], BF16))
        self.E2 = st.enter_context(nc.sbuf_tensor("E2", [2, 128], BF16))
        self.STT = st.enter_context(nc.sbuf_tensor("STT", [128, 32], F32))
        self.CONSTS = st.enter_context(nc.sbuf_tensor("CONSTS", [128, 8], F32))
        nc.const_aps.register(F32, 0.0, self.CONSTS.ap()[:, 0:1])
        nc.const_aps.register(F32, 1e-5, self.CONSTS.ap()[:, 1:2])
        nc.const_aps.register(F32, 1.0, self.CONSTS.ap()[:, 2:3])
        nc.const_aps.register(F32, 1e-30, self.CONSTS.ap()[:, 3:4])
        nc.const_aps.register(F32, -2.0794415416798357, self.CONSTS.ap()[:, 4:5])
        nc.const_aps.register(F32, 44.3614195558, self.CONSTS.ap()[:, 5:6])
        self.TMP = st.enter_context(nc.sbuf_tensor("TMP", [128, 4352], U8))
        self.STG = st.enter_context(nc.sbuf_tensor("STG", [128, 4096], U8))
        self.AR = st.enter_context(nc.sbuf_tensor("AR", [128, A_END], U8))
        self.WB = st.enter_context(nc.sbuf_tensor("WB", [128, 8192], U8))
        self.PSA = st.enter_context(nc.psum_tensor("PSA", [128, 2048], F32))
        self.PSB = st.enter_context(nc.psum_tensor("PSB", [128, 2048], F32))

        c = self.CST.ap()
        self.PERM = c[:, 0:128]
        self.MD0 = c[:, 128:256]
        self.M0 = c[:, 256:384]
        self.M1 = c[:, 384:512]
        self.M1E = c[:, 512:640]
        self.IDT = c[:, 640:768]
        self.ONESB = c[:, 768:896]

        def sem(name):
            return Sem(st.enter_context(nc.semaphore(name)))
        self.dmas = sem("dmas")
        self.dmag = sem("dmag")
        self.spe = sem("spe")
        self.sact = sem("sact")
        self.sdve = sem("sdve")
        self.dstg = [sem("dstg0"), sem("dstg1")]
        self.dx = sem("dx")
        self.dcc = sem("dcc")
        self.ddm = sem("ddm")

    def reset(self):
        for s in (self.dmas, self.dmag, self.spe, self.sact, self.sdve,
                  self.dstg[0], self.dstg[1], self.dx, self.dcc, self.ddm):
            s.n = 0
        self._stgj = 0
        self._stg_guard = {}

    def vw(self, off, dt, shape, arena=None):
        ar = self.AR if arena is None else arena
        n = 1
        for x in shape[1:]:
            n *= x
        nb = n * _DT_SZ[dt]
        ap = ar.ap()[:, off:off + nb].bitcast(dt)
        if len(shape) > 2:
            dims = " ".join(f"d{i}" for i in range(1, len(shape)))
            ap = ap.rearrange(f"p ({dims}) -> p {dims}",
                              **{f"d{i}": shape[i] for i in range(1, len(shape))})
        return ap

    def bA(self, i, w=512):
        return self.PSA.ap()[:, i * 512:i * 512 + w]

    def bB(self, i, w=512):
        return self.PSB.ap()[:, i * 512:i * 512 + w]

    def program(self):
        self.reset()
        self.load_consts_and_caches()
        if globals().get("_PREPBAR"):
            for eng in ("tensor", "vector", "scalar", "sync"):
                self.wait(eng, self.dmag)
        for seq in range(NSEQ):
            self.seq_program(seq)
        self.wait("sync", self.dmas)
        self.wait("sync", self.dstg[0])
        self.wait("sync", self.dstg[1])
        self.wait("sync", self.dx)
        self.wait("sync", self.dcc)
        self.wait("sync", self.ddm)
        self.wait("vector", self.dmas)

    def load_consts_and_caches(self):
        d = self.d
        self.y(lambda e: e.dma_start(self.CST.ap(), d["cst"]), [(self.dmas, 16)])
        self.y(lambda e: e.dma_start(self.CB.ap(), d["cb"]), [(self.dmas, 16)])
        self.cstmark = self.dmas.n
        self.wait("vector", self.dmas, self.cstmark)
        self.v(lambda e: e.memset(self.CONSTS.ap()[:, 0:1], 0.0), [(self.sdve, 1)])
        self.v(lambda e: e.memset(self.CONSTS.ap()[:, 1:2], 1e-5), [(self.sdve, 1)])
        self.v(lambda e: e.memset(self.CONSTS.ap()[:, 2:3], 1.0), [(self.sdve, 1)])
        self.v(lambda e: e.memset(self.CONSTS.ap()[:, 3:4], 1e-30), [(self.sdve, 1)])
        self.v(lambda e: e.memset(self.CONSTS.ap()[:, 4:5], -2.0794415416798357),
               [(self.sdve, 1)])
        self.v(lambda e: e.memset(self.CONSTS.ap()[:, 5:6], 44.3614195558),
               [(self.sdve, 1)])
        V = self.vw(A_V, BF16, [128, 16, H0, 65])
        self.v(lambda e, V=V: e.memset(V[:, :, :, 64:65], 1.0), [(self.sdve, 1)])
        self.onesmark = self.sdve.n
        self.wait("scalar", self.dmas, self.cstmark)
        BQ = self.vw(T_A, E5, [128, 40], arena=self.TMP)
        self.s(lambda e, BQ=BQ: e.activation(
            BQ[:], self.CB.ap()[:, 46:86], AF.Copy), [(self.sact, 1)])
        self.wait("scalar", self.sact)
        self.s(lambda e, BQ=BQ: e.activation(
            self.CB.ap()[:, 46:86], BQ[:], AF.Copy), [(self.sact, 1)])
        self.cbq_mark = self.sact.n

    def seq_program(self, seq):
        self.load_x(seq)
        if _STAGE in ("copy", "prep"):
            self.wait("sync", self.dx, self.xmark)
            self.y(lambda e, seq=seq: e.dma_start(
                self.d["out"][seq].rearrange("(c p) t -> p c t", p=128),
                self.XR.ap()), [(self.dmas, 16)])
            return
        self.stats(seq)
        if _STAGE == "xn":
            self.dbg_store_xn(seq)
            return
        self.l0_p2(seq)
        if _STAGE == "kb":
            self.wait("sync", self.sdve)
            o = self.d["out"][seq].rearrange("(c p) t -> p c t", p=128)
            self.y(lambda e, o=o: e.dma_start(o[:, 0, 0:32], self.STT.ap()),
                   [(self.dmas, 16)])
            self.wait("vector", self.dmas)
            self.wait("scalar", self.dmas)
            self.wait("tensor", self.dmas)
            return
        if _STAGE == "negm":
            self.l0_p3(seq, 0)
            return
        for qc in range(4):
            self.l0_p3(seq, qc)
        if _STAGE in ("l0", "g0", "s0"):
            self.store_out(seq)
            return
        self.stats(seq)
        self.l1_p2(seq)
        if _STAGE == "l1p2":
            self.store_out(seq)
            return
        for qc in range(4):
            self.l1_p3(seq, qc)
        self.store_out(seq)

    def load_x(self, seq):
        xin = self.d["x"][seq].rearrange("(c p) t -> p c t", p=128)
        if seq > 0:
            self.wait("sync", self.dmas)
        self.y(lambda e, xin=xin: e.dma_start(self.XR.ap(), xin), [(self.dx, 16)])
        self.xmark = self.dx.n

    def stats(self, seq):
        XSQ = [self.vw(A_U + i * 1024, BF16, [128, 512]) for i in range(2)]
        XBF = [self.vw(A_U + 2048 + i * 1024, BF16, [128, 512]) for i in range(2)]
        MB = [self.vw(A_U + 4096 + i * 2048, F32, [128, 512]) for i in range(4)]
        TV = [self.vw(A_U + 12288 + i * 2048, F32, [128, 512]) for i in range(4)]
        B1 = self.vw(A_B1, F32, [128, S])
        B2 = self.vw(A_B2, F32, [128, S])
        xr = self.XR.ap()
        self.wait("scalar", self.dx, self.xmark)
        self.wait("scalar", self.sdve, self.onesmark)
        self.wait("tensor", self.dx, self.xmark)
        self.wait("tensor", self.dmas, self.cstmark)
        self.wait("tensor", self.sdve)
        self.wait("scalar", self.sdve)
        rd_marks = []
        i = 0
        for c in range(NCH):
            for qs in range(4):
                if i >= 2:
                    self.wait("scalar", self.spe, rd_marks[i - 2])
                self.s(lambda e, c=c, qs=qs, i=i: e.activation(
                    XSQ[i % 2][:], xr[:, c, qs * 512:qs * 512 + 512], AF.Square))
                self.s(lambda e, c=c, qs=qs, i=i: e.activation(
                    XBF[i % 2][:], xr[:, c, qs * 512:qs * 512 + 512], AF.Copy),
                    [(self.sact, 1)])
                sqm = self.sact.n
                self.wait("tensor", self.sact, sqm)
                self.t(lambda e, i=i, qs=qs, c=c: e.matmul(
                    self.bA(qs), self.ONESB, XBF[i % 2][:],
                    start=(c == 0), stop=(c == NCH - 1)))
                self.t(lambda e, i=i, qs=qs, c=c: e.matmul(
                    self.bB(qs), self.ONESB, XSQ[i % 2][:],
                    start=(c == 0), stop=(c == NCH - 1)), [(self.spe, 1)])
                rd_marks.append(self.spe.n)
                i += 1
        self.wait("vector", self.spe)
        for qs in range(4):
            self.v(lambda e, qs=qs: e.tensor_scalar(
                MB[qs][:], self.bA(qs), 1.0 / D, None, OP.mult))
            self.v(lambda e, qs=qs: e.tensor_scalar(
                TV[qs][:], self.bB(qs), 1.0 / D, None, OP.mult))
            self.v(lambda e, qs=qs: e.tensor_tensor(
                XSQ[0][:], MB[qs][:], MB[qs][:], OP.mult))
            self.v(lambda e, qs=qs: e.tensor_tensor(
                TV[qs][:], TV[qs][:], XSQ[0][:], OP.subtract), [(self.sdve, 1)])
            self.wait("scalar", self.sdve)
            self.s(lambda e, qs=qs: e.activation(XSQ[1][:], TV[qs][:], AF.Ln, bias=1e-5))
            self.s(lambda e, qs=qs: e.activation(
                B2[:, qs * 512:qs * 512 + 512], XSQ[1][:], AF.Exp, scale=-0.5),
                [(self.sact, 1)])
            self.wait("vector", self.sact)
            self.v(lambda e, qs=qs: e.tensor_tensor(
                B1[:, qs * 512:qs * 512 + 512], MB[qs][:],
                B2[:, qs * 512:qs * 512 + 512], OP.mult), [(self.sdve, 1)])
        self.stats_done = self.sdve.n

    def regen_xn(self, qs):
        XN = self.vw(A_XN, BF16, [128, NCH, 512])
        B1 = self.vw(A_B1, F32, [128, S])
        B2 = self.vw(A_B2, F32, [128, S])
        TMPF = self.vw(T_C, F32, [128, 512], arena=self.TMP)
        xr = self.XR.ap()
        sl = slice(qs * 512, qs * 512 + 512)
        for c in range(NCH):
            self.v(lambda e, c=c, sl=sl: e.tensor_tensor(
                TMPF[:], xr[:, c, sl], B2[:, sl], OP.mult))
            self.v(lambda e, c=c, sl=sl: e.tensor_tensor(
                XN[:, c, :], TMPF[:], B1[:, sl], OP.subtract), [(self.sdve, 1)])
        return self.sdve.n

    def wload(self, src_name, m0, buf_off, guard_mark=None, odt=BF16):
        dst = self.vw(buf_off, odt, [128, 1, 8, 128], arena=self.WB)
        idt = I8 if src_name in ("wgT", "wuT") else F32
        esz = _DT_SZ[idt]
        osz = _DT_SZ[odt]
        for half in range(2):
            j = self._stgj
            soff = (j % 2) * 2048
            stg = self.STG.ap()[:, soff:soff + 512 * esz].bitcast(idt)
            src = self.d[src_name][m0][:, half * 4:half * 4 + 4, :].rearrange(
                "p c n -> p (c n)")
            g = self._stg_guard.get(j % 2)
            if g is not None:
                self.wait("sync", self.sact, g)
            if half == 0 and guard_mark is not None:
                self.wait("sync", self.spe, guard_mark)
            ds = self.dstg[j % 2]
            self.y(lambda e, stg=stg, src=src: e.dma_start(stg, src),
                   [(ds, 16)])
            dm = ds.n
            self.wait("scalar", ds, dm)
            dv = self.vw(buf_off + half * 512 * osz, odt, [128, 512],
                         arena=self.WB)
            self.s(lambda e, dv=dv, stg=stg: e.activation(dv, stg, AF.Copy),
                   [(self.sact, 1)])
            self._stg_guard[j % 2] = self.sact.n
            self._stgj += 1
        return dst, self.sact.n

    def l0_p2(self, seq):
        d = self.d
        K = self.vw(A_K, BF16, [128, NCH, S])
        V = self.vw(A_V, BF16, [128, 16, H0, 65])
        CC = self.vw(A_CCSS, BF16, [128, 2 * S])
        XN = self.vw(A_XN, BF16, [128, NCH, 512])
        WVH = self.vw(A_U + 4096, BF16, [128, 8, D])
        Ktmp = self.vw(T_A, BF16, [128, 512], arena=self.TMP)
        T1b = self.vw(T_B, BF16, [128, 512], arena=self.TMP)
        self.wait("sync", self.sdve, self.stats_done)
        self.y(lambda e: e.dma_start(CC, d["ccss"]), [(self.dcc, 16)])
        ccmark = self.dcc.n
        self.wait("scalar", self.sdve, self.stats_done)
        for c in range(NCH):
            for eh in range(2):
                j = self._stgj
                soff = (j % 2) * 2048
                stg = self.STG.ap()[:, soff:soff + 2048].bitcast(F32)
                g = self._stg_guard.get(j % 2)
                if g is not None:
                    self.wait("sync", self.sact, g)
                ds = self.dstg[j % 2]
                self.y(lambda e, stg=stg, c=c, eh=eh: e.dma_start(
                    stg, d["wvT"][c * 128:(c + 1) * 128,
                                  eh * 512:eh * 512 + 512]), [(ds, 16)])
                dm = ds.n
                self.wait("scalar", ds, dm)
                self.s(lambda e, WVH=WVH, c=c, eh=eh, stg=stg: e.activation(
                    WVH[:, c, eh * 512:eh * 512 + 512], stg, AF.Copy),
                    [(self.sact, 1)])
                self._stg_guard[j % 2] = self.sact.n
                self._stgj += 1
        wvmark = self.sact.n
        self.wait("vector", self.dcc, ccmark)
        guard = {}
        g0 = self.spe.n
        for qs in range(4):
            xnm = self.regen_xn(qs)
            self.wait("tensor", self.sdve, xnm)
            sl = slice(qs * 512, qs * 512 + 512)
            slS = slice(S + qs * 512, S + qs * 512 + 512)
            for m in range(NCH):
                wb, wm = self.wload("wkT", m, (m % 2) * 2048,
                                    guard.get(m % 2, g0))
                self.wait("tensor", self.sact, wm)
                for c in range(NCH):
                    self.t(lambda e, m=m, c=c, wb=wb, XN=XN: e.matmul(
                        self.bA(m % 2), wb[:, 0, c, :], XN[:, c, :],
                        start=(c == 0), stop=(c == NCH - 1)),
                        [(self.spe, 1)] if c == NCH - 1 else ())
                guard[m % 2] = self.spe.n
                self.wait("vector", self.spe)
                self.v(lambda e, m=m, Ktmp=Ktmp: e.tensor_copy(
                    Ktmp[:], self.bA(m % 2)), [(self.sdve, 1)])
                self.wait("tensor", self.sdve)
                self.t(lambda e, m=m, Ktmp=Ktmp: e.matmul(
                    self.bA(2 + m % 2), self.PERM, Ktmp[:], start=True, stop=True),
                    [(self.spe, 1)])
                self.wait("vector", self.spe)
                self.v(lambda e, sl=sl, Ktmp=Ktmp, T1b=T1b, CC=CC: e.tensor_tensor(
                    T1b[:], Ktmp[:], CC[:, sl], OP.mult))
                self.v(lambda e, m=m, slS=slS, Ktmp=Ktmp, CC=CC: e.tensor_tensor(
                    Ktmp[:], self.bA(2 + m % 2), CC[:, slS], OP.mult))
                self.v(lambda e, m=m, sl=sl, K=K, T1b=T1b, Ktmp=Ktmp: e.tensor_tensor(
                    K[:, m, sl], T1b[:], Ktmp[:], OP.add), [(self.sdve, 1)])
                self.wait("tensor", self.sdve)
            if qs == 0:
                self.wait("tensor", self.sact, wvmark)
            for tl in range(4):
                kt = qs * 4 + tl
                for ec in range(2):
                    pb = (tl * 2 + ec) % 2
                    for c in range(NCH):
                        self.t(lambda e, tl=tl, ec=ec, c=c, pb=pb, XN=XN, WVH=WVH: e.matmul(
                            self.bB(pb), XN[:, c, tl * 128:tl * 128 + 128],
                            WVH[:, c, ec * 512:ec * 512 + 512],
                            start=(c == 0), stop=(c == NCH - 1)),
                            [(self.spe, 1)] if c == NCH - 1 else ())
                    self.wait("vector", self.spe)
                    self.v(lambda e, kt=kt, ec=ec, pb=pb, V=V: e.tensor_copy(
                        V[:, kt, ec * 8:ec * 8 + 8, 0:64],
                        self.bB(pb).rearrange("p (h n) -> p h n", h=8)),
                        [(self.sdve, 1)])
                    self.wait("tensor", self.sdve)
        XSQF = self.vw(A_U, BF16, [128, 512])
        STT = self.STT.ap()
        self.wait("scalar", self.sdve)
        for c in range(NCH):
            for qs in range(4):
                self.s(lambda e, c=c, qs=qs, K=K, XSQF=XSQF: e.activation(
                    XSQF[:], K[:, c, qs * 512:qs * 512 + 512], AF.Square),
                    [(self.sact, 1)])
                self.wait("tensor", self.sact)
                self.t(lambda e, qs=qs, XSQF=XSQF: e.matmul(
                    self.bB(qs), self.ONESB, XSQF[:], start=True, stop=True),
                    [(self.spe, 1)])
                self.wait("scalar", self.spe)
            self.wait("vector", self.spe)
            self.v(lambda e, c=c, STT=STT: e.tensor_reduce(
                STT[:, c:c + 1], self.PSB.ap(), AX.X, OP.max), [(self.sdve, 1)])
            self.wait("tensor", self.sdve)

    def l0_p3(self, seq, qc):
        d = self.d
        K = self.vw(A_K, BF16, [128, NCH, S])
        V = self.vw(A_V, BF16, [128, 16, H0, 65])
        CC = self.vw(A_CCSS, BF16, [128, 2 * S])
        XN = self.vw(A_XN, BF16, [128, NCH, 512])
        Q = self.vw(A_Q, BF16, [128, NCH, 512])
        G = self.vw(A_G, BF16, [128, NCH, 512])
        H = self.vw(A_U, BF16, [128, 24, 512])
        PT = [self.vw(A_U + i * 4096, BF16, [128, 2048]) for i in range(2)]
        ATM = self.vw(A_U + 8192, BF16, [128, 4, D])
        AFM = self.vw(A_U + 16384, BF16, [128, 8, 512])
        Ktmp = self.vw(T_A, BF16, [128, 512], arena=self.TMP)
        T1b = self.vw(T_B, BF16, [128, 512], arena=self.TMP)
        TNH = self.vw(T_A, BF16, [128, 512], arena=self.TMP)
        U2 = self.vw(T_B, BF16, [128, 512], arena=self.TMP)
        AT = self.vw(T_C, F32, [128, 512], arena=self.TMP)
        XSQF = self.vw(T_C, BF16, [128, 256], arena=self.TMP)
        LC = self.vw(T_D, F32, [128, 1], arena=self.TMP)
        REC = self.vw(T_D + 192, F32, [128, 1], arena=self.TMP)
        STT = self.STT.ap()
        sl = slice(qc * 512, qc * 512 + 512)
        slS = slice(S + qc * 512, S + qc * 512 + 512)

        self.regen_xn(qc)
        xnm = self.sdve.n
        self.wait("tensor", self.sdve, xnm)
        guard = {}
        g0 = self.spe.n
        for m in range(NCH):
            wb, wm = self.wload("wqT", m, (m % 2) * 2048, guard.get(m % 2, g0))
            self.wait("tensor", self.sact, wm)
            for c in range(NCH):
                self.t(lambda e, m=m, c=c, wb=wb, XN=XN: e.matmul(
                    self.bA(m % 2), wb[:, 0, c, :], XN[:, c, :],
                    start=(c == 0), stop=(c == NCH - 1)),
                    [(self.spe, 1)] if c == NCH - 1 else ())
            guard[m % 2] = self.spe.n
            self.wait("vector", self.spe)
            self.v(lambda e, m=m, Ktmp=Ktmp: e.tensor_copy(Ktmp[:], self.bA(m % 2)),
                   [(self.sdve, 1)])
            self.wait("tensor", self.sdve)
            self.t(lambda e, m=m, Ktmp=Ktmp: e.matmul(
                self.bA(2 + m % 2), self.PERM, Ktmp[:], start=True, stop=True),
                [(self.spe, 1)])
            self.wait("vector", self.spe)
            self.v(lambda e, sl=sl, Ktmp=Ktmp, T1b=T1b, CC=CC: e.tensor_tensor(
                T1b[:], Ktmp[:], CC[:, sl], OP.mult))
            self.v(lambda e, m=m, slS=slS, Ktmp=Ktmp, CC=CC: e.tensor_tensor(
                Ktmp[:], self.bA(2 + m % 2), CC[:, slS], OP.mult))
            self.v(lambda e, m=m, Q=Q, T1b=T1b, Ktmp=Ktmp: e.tensor_tensor(
                Q[:, m, :], T1b[:], Ktmp[:], OP.add), [(self.sdve, 1)])
            self.wait("tensor", self.sdve)
        self.wait("scalar", self.sdve)
        for c in range(NCH):
            for hf in range(2):
                self.s(lambda e, c=c, hf=hf, Q=Q, XSQF=XSQF: e.activation(
                    XSQF[:], Q[:, c, hf * 256:hf * 256 + 256], AF.Square),
                    [(self.sact, 1)])
                self.wait("tensor", self.sact)
                self.t(lambda e, hf=hf, XSQF=XSQF: e.matmul(
                    self.PSB.ap()[:, hf * 256:hf * 256 + 256], self.ONESB,
                    XSQF[:], start=True, stop=True), [(self.spe, 1)])
                self.wait("scalar", self.spe)
            self.wait("vector", self.spe)
            self.v(lambda e, c=c, STT=STT: e.tensor_reduce(
                STT[:, 8 + c:9 + c], self.PSB.ap()[:, 0:512], AX.X, OP.max),
                [(self.sdve, 1)])
            self.wait("tensor", self.sdve)
            self.wait("vector", self.sdve)
            self.v(lambda e, c=c, STT=STT: e.tensor_tensor(
                STT[:, 16 + c:17 + c], STT[:, 8 + c:9 + c], STT[:, c:c + 1],
                OP.add), [(self.sdve, 1)])
            self.wait("vector", self.sdve)
            self.v(lambda e, c=c, STT=STT: e.tensor_scalar(
                STT[:, 16 + c:17 + c], STT[:, 16 + c:17 + c], -1.0 / 16.0,
                44.3614195558, OP.mult, op1=OP.add), [(self.sdve, 1)])
        negm_mark = self.sdve.n
        if _STAGE == "negm":
            self.wait("sync", self.sdve, negm_mark)
            o = self.d["out"][seq].rearrange("(c p) t -> p c t", p=128)
            self.y(lambda e, o=o: e.dma_start(o[:, 0, 0:32], self.STT.ap()),
                   [(self.dmas, 16)])
            self.y(lambda e, o=o: e.dma_start(o[:, 1, 0:8], self.CONSTS.ap()),
                   [(self.dmas, 16)])
            self.wait("vector", self.dmas)
            self.wait("scalar", self.dmas)
            self.wait("tensor", self.dmas)
            return
        gW = {}
        g0f = self.spe.n
        for m in range(24):
            wbg, wmg = self.wload("wgT", m, (m % 2) * 2048, gW.get(m % 2, g0f))
            wbu, wmu = self.wload("wuT", m, 4096 + (m % 2) * 2048, None)
            self.wait("tensor", self.sact, wmu)
            gb, ub = (m % 2) * 2, (m % 2) * 2 + 1
            for c in range(NCH):
                self.t(lambda e, c=c, gb=gb, wbg=wbg, XN=XN: e.matmul(
                    self.bA(gb), wbg[:, 0, c, :], XN[:, c, :],
                    start=(c == 0), stop=(c == NCH - 1)))
                self.t(lambda e, c=c, ub=ub, wbu=wbu, XN=XN: e.matmul(
                    self.bA(ub), wbu[:, 0, c, :], XN[:, c, :],
                    start=(c == 0), stop=(c == NCH - 1)),
                    [(self.spe, 1)] if c == NCH - 1 else ())
            gW[m % 2] = self.spe.n
            self.wait("scalar", self.spe)
            if m > 0:
                self.wait("scalar", self.sdve)
            self.s(lambda e, gb=gb, TNH=TNH: e.activation(
                TNH[:], self.bA(gb), AF.Tanh, scale=0.5 * INV127), [(self.sact, 1)])
            self.wait("vector", self.sact)
            self.v(lambda e, TNH=TNH, U2=U2: e.tensor_scalar(
                U2[:], TNH[:], 1.0, 0.5 * INV127 * INV127, OP.add, op1=OP.mult))
            self.v(lambda e, gb=gb, U2=U2, AT=AT: e.tensor_tensor(
                AT[:], U2[:], self.bA(gb), OP.mult))
            self.v(lambda e, m=m, ub=ub, H=H, AT=AT: e.tensor_tensor(
                H[:, m, :], AT[:], self.bA(ub), OP.mult), [(self.sdve, 1)])
            self.wait("tensor", self.sdve)
        wd_guard = self.spe.n
        for ch in range(24):
            if ch % 2 == 0:
                wdld = self.vw((ch // 2 % 2) * 4096, BF16, [128, 2, D], arena=self.WB)
                j = self._stgj
                soff = (j % 2) * 2048
                stg = self.STG.ap()[:, soff:soff + 2048].bitcast(I8)
                g = self._stg_guard.get(j % 2)
                if g is not None:
                    self.wait("sync", self.sact, g)
                if wd_guard is not None:
                    self.wait("sync", self.spe, wd_guard)
                ds = self.dstg[j % 2]
                self.y(lambda e, stg=stg, ch=ch: e.dma_start(
                    stg, d["wdT"][ch // 2].rearrange("p h n -> p (h n)")),
                    [(ds, 16)])
                dm = ds.n
                self.wait("scalar", ds, dm)
                wflat = self.vw((ch // 2 % 2) * 4096, BF16, [128, 2048],
                                arena=self.WB)
                self.s(lambda e, w=wflat, stg=stg: e.activation(
                    w, stg, AF.Copy), [(self.sact, 1)])
                self._stg_guard[j % 2] = self.sact.n
                self._stgj += 1
                wdm = self.sact.n
                self.wait("tensor", self.sact, wdm)
            for dt in range(8):
                bank = self.bA(dt) if dt < 4 else self.bB(dt - 4)
                self.t(lambda e, ch=ch, dt=dt, bank=bank, w=wdld, H=H: e.matmul(
                    bank, w[:, ch % 2, dt * 128:dt * 128 + 128], H[:, ch, :],
                    start=(ch == 0), stop=(ch == 23)),
                    [(self.spe, 1)] if dt == 7 else ())
            if ch % 2 == 1:
                wd_guard = self.spe.n
        self.wait("scalar", self.spe)
        for dt in range(8):
            bank = self.bA(dt) if dt < 4 else self.bB(dt - 4)
            if dt > 0:
                self.wait("scalar", self.sdve)
            self.s(lambda e, bank=bank, TNH=TNH: e.activation(
                TNH[:], bank, AF.Tanh, scale=0.5 * INV127), [(self.sact, 1)])
            self.wait("vector", self.sact)
            self.v(lambda e, TNH=TNH, U2=U2: e.tensor_scalar(
                U2[:], TNH[:], 1.0, 0.5 * INV127, OP.add, op1=OP.mult))
            self.v(lambda e, dt=dt, bank=bank, G=G, U2=U2: e.tensor_tensor(
                G[:, dt, :], U2[:], bank, OP.mult), [(self.sdve, 1)])
        gmark = self.sdve.n
        if _STAGE == "s0":
            self.wait("vector", self.sdve, gmark)
            for dt in range(8):
                self.v(lambda e, dt=dt, G=G: e.memset(G[:, dt, :], 1.0), [(self.sdve, 1)])
            gmark = self.sdve.n
        if _STAGE == "g0":
            self.wait("vector", self.sdve, gmark)
            for dt in range(8):
                self.v(lambda e, dt=dt, sl=sl, G=G: e.tensor_copy(
                    self.XR.ap()[:, dt, sl], G[:, dt, :]), [(self.sdve, 1)])
            return
        self.wait("tensor", self.sdve, gmark)
        self.wait("tensor", self.sdve, negm_mark)
        ngrp = qc + 1
        evac_mark = {}
        for h in range(H0):
            hc, hb = h // 2, (h % 2) * 64
            po = (h % 2) * 65
            if h >= 2:
                self.wait("tensor", self.sdve, evac_mark[h - 2])
            qt_first = {}
            qt_last = {}
            for g_ in range(ngrp):
                nkt_ = min(4, 4 * qc + 4 - g_ * 4)
                for qt_ in range(4):
                    for j_ in range(nkt_):
                        kt_ = g_ * 4 + j_
                        if kt_ <= qc * 4 + qt_:
                            if qt_ not in qt_first:
                                qt_first[qt_] = (g_, j_)
                            qt_last[qt_] = (g_, j_)
            for g in range(ngrp):
                nkt = min(4, 4 * qc + 4 - g * 4)
                for j in range(nkt):
                    kt = g * 4 + j
                    self.t(lambda e, hc=hc, hb=hb, kt=kt, j=j, K=K, Q=Q: e.matmul(
                        self.bA(j), K[hb:hb + 64, hc, kt * 128:kt * 128 + 128],
                        Q[hb:hb + 64, hc, :], start=True, stop=True),
                        [(self.spe, 1)] if j == nkt - 1 else ())
                self.wait("scalar", self.spe)
                self.s(lambda e, hc=hc, g=g, nkt=nkt, PT=PT: e.activation(
                    PT[g % 2][:, 0:nkt * 512], self.PSA.ap()[:, 0:nkt * 512],
                    AF.Exp, bias=self.STT.ap()[:, 16 + hc:17 + hc], scale=0.125),
                    [(self.sact, 1)])
                self.wait("vector", self.sact)
                self.wait("tensor", self.sact)
                did_mask = False
                for qt in range(4):
                    ktd = qc * 4 + qt
                    if g * 4 <= ktd < g * 4 + nkt:
                        j = ktd - g * 4
                        off = j * 512 + qt * 128
                        self.v(lambda e, g=g, off=off, PT=PT: e.tensor_tensor(
                            PT[g % 2][:, off:off + 128], PT[g % 2][:, off:off + 128],
                            self.MD0, OP.mult), [(self.sdve, 1)])
                        did_mask = True
                if did_mask:
                    self.wait("tensor", self.sdve)
                pv_ops = []
                for qt in range(4):
                    for j in range(nkt):
                        kt = g * 4 + j
                        if kt <= qc * 4 + qt:
                            pv_ops.append((qt, j, kt))
                for idx, (qt, j, kt) in enumerate(pv_ops):
                    st = qt_first[qt] == (g, j)
                    last = qt_last[qt] == (g, j)
                    self.t(lambda e, g=g, j=j, qt=qt, kt=kt, h=h, st=st, last=last,
                           po=po, PT=PT, V=V: e.matmul(
                        self.PSB.ap()[:, qt * 512 + po:qt * 512 + po + 65],
                        PT[g % 2][:, j * 512 + qt * 128:j * 512 + qt * 128 + 128],
                        V[:, kt, h, :], start=st, stop=last, skip_group_check=True),
                        [(self.spe, 1)] if idx == len(pv_ops) - 1 else ())
            self.wait("vector", self.spe)
            LC4 = self.vw(T_D, F32, [128, 4], arena=self.TMP)
            REC4 = self.vw(T_D + 128, F32, [128, 4], arena=self.TMP)
            for qt in range(4):
                pvr = self.PSB.ap()[:, qt * 512 + po:qt * 512 + po + 65]
                self.v(lambda e, qt=qt, pvr=pvr, LC4=LC4: e.tensor_scalar(
                    LC4[:, qt:qt + 1], pvr[:, 64:65], 1e-30, None, OP.max),
                    [(self.sdve, 1)] if qt == 3 else ())
            self.wait("vector", self.sdve)
            self.v(lambda e, LC4=LC4, REC4=REC4: e.reciprocal(REC4[:], LC4[:]),
                   [(self.sdve, 1)])
            self.wait("vector", self.sdve)
            for qt in range(4):
                pvr = self.PSB.ap()[:, qt * 512 + po:qt * 512 + po + 65]
                self.v(lambda e, qt=qt, h=h, pvr=pvr, ATM=ATM, REC4=REC4:
                       e.tensor_scalar(
                    ATM[:, qt, h * 64:h * 64 + 64], pvr[:, 0:64],
                    REC4[:, qt:qt + 1], None, OP.mult), [(self.sdve, 1)])
            evac_mark[h] = self.sdve.n
        self.wait("tensor", self.sdve, evac_mark[H0 - 1])
        for qt in range(4):
            for eg in range(2):
                for j in range(4):
                    ec = eg * 4 + j
                    po = (2 + eg) * 512 + j * 128
                    self.t(lambda e, qt=qt, ec=ec, po=po, ATM=ATM: e.matmul(
                        self.PSA.ap()[:, po:po + 128],
                        ATM[:, qt, ec * 128:ec * 128 + 128], self.IDT,
                        start=True, stop=True),
                        [(self.spe, 1)] if j == 3 else ())
                self.wait("vector", self.spe)
                self.v(lambda e, qt=qt, eg=eg, AFM=AFM: e.tensor_copy(
                    AFM[:, eg * 4:eg * 4 + 4, qt * 128:qt * 128 + 128],
                    self.bA(2 + eg).rearrange("p (j n) -> p j n", j=4)),
                    [(self.sdve, 1)])
                self.wait("tensor", self.sdve)
        guard = {}
        g0o = self.spe.n
        for m in range(NCH):
            wb, wm = self.wload("woT", m, (m % 2) * 2048, guard.get(m % 2, g0o))
            self.wait("tensor", self.sact, wm)
            for ec in range(NCH):
                self.t(lambda e, m=m, ec=ec, wb=wb, AFM=AFM: e.matmul(
                    self.bA(m % 2), wb[:, 0, ec, :], AFM[:, ec, :],
                    start=(ec == 0), stop=(ec == NCH - 1)),
                    [(self.spe, 1)] if ec == NCH - 1 else ())
            guard[m % 2] = self.spe.n
            self.wait("vector", self.spe)
            self.v(lambda e, m=m, AT=AT, G=G: e.tensor_tensor(
                AT[:], self.bA(m % 2), G[:, m, :], OP.mult))
            self.v(lambda e, m=m, sl=sl, AT=AT: e.tensor_tensor(
                self.XR.ap()[:, m, sl], self.XR.ap()[:, m, sl], AT[:], OP.add),
                [(self.sdve, 1)])
            self.wait("tensor", self.sdve)

    def l1_p2(self, seq):
        d = self.d
        K = self.vw(A_K, BF16, [128, NCH, S])
        V = self.vw(A_V, BF16, [128, 16, H0, 65])
        XN = self.vw(A_XN, BF16, [128, NCH, 512])
        WVH = self.vw(A_U + 4096, BF16, [128, 8, D])
        ML = self.vw(A_CCSS, BF16, [64, S])
        POOLF = self.vw(A_CCSS + 4096, F32, [128, NCH, LAT])
        POOLB = self.vw(A_CCSS + 6144, BF16, [128, NCH, LAT])
        RKF = self.vw(A_CCSS + 7168, BF16, [128, 2, LAT])
        RV = self.vw(A_CCSS + 7424, BF16, [64, LH, 65])
        RVF = self.vw(T_A, BF16, [128, 2, LAT], arena=self.TMP)
        CB = self.CB.ap()
        self.y(lambda e: e.dma_start(ML[0:64, :], d["ml"]), [(self.dcc, 16)])
        self.y(lambda e: e.dma_start(self.E2.ap(), d["e2"]), [(self.dcc, 16)])
        mlmark = self.dcc.n
        if seq == 0:
            j = self._stgj
            stg = self.STG.ap()[:, (j % 2) * 2048:(j % 2) * 2048 + 16].bitcast(F32)
            g = self._stg_guard.get(j % 2)
            if g is not None:
                self.wait("sync", self.sact, g)
            self.y(lambda e, stg=stg: e.dma_start(stg, d["glgr"]),
                   [(self.dstg[j % 2], 16)])
            dm = self.dstg[j % 2].n
            self.wait("scalar", self.dstg[j % 2], dm)
            self.s(lambda e, stg=stg: e.activation(
                self.GLGR.ap(), stg, AF.Copy), [(self.sact, 1)])
            self._stg_guard[j % 2] = self.sact.n
            self._stgj += 1
            for i in range(3):
                j = self._stgj
                stg = self.STG.ap()[0:1, (j % 2) * 2048:(j % 2) * 2048 + 2048]
                stg = stg.bitcast(F32)
                g = self._stg_guard.get(j % 2)
                if g is not None:
                    self.wait("sync", self.sact, g)
                self.y(lambda e, stg=stg, i=i: e.dma_start(
                    stg, d["cr"][0:1, i * 512:i * 512 + 512]),
                    [(self.dstg[j % 2], 16)])
                dm = self.dstg[j % 2].n
                self.wait("scalar", self.dstg[j % 2], dm)
                self.s(lambda e, stg=stg, i=i: e.activation(
                    self.CRB.ap()[0:1, i * 512:i * 512 + 512], stg, AF.Copy),
                    [(self.sact, 1)])
                self._stg_guard[j % 2] = self.sact.n
                self._stgj += 1
        self.consts1 = self.sact.n
        self.wait("vector", self.sdve, self.stats_done)
        self.v(lambda e, RV=RV: e.memset(RV[0:64, :, 64:65], 1.0), [(self.sdve, 1)])
        self.wait("scalar", self.sdve, self.stats_done)
        for c in range(NCH):
            for eh in range(2):
                j = self._stgj
                soff = (j % 2) * 2048
                stg = self.STG.ap()[:, soff:soff + 2048].bitcast(F32)
                g = self._stg_guard.get(j % 2)
                if g is not None:
                    self.wait("sync", self.sact, g)
                ds = self.dstg[j % 2]
                self.y(lambda e, stg=stg, c=c, eh=eh: e.dma_start(
                    stg, d["qkvV"][c * 128:(c + 1) * 128,
                                   eh * 512:eh * 512 + 512]), [(ds, 16)])
                dm = ds.n
                self.wait("scalar", ds, dm)
                self.s(lambda e, WVH=WVH, c=c, eh=eh, stg=stg: e.activation(
                    WVH[:, c, eh * 512:eh * 512 + 512], stg, AF.Copy),
                    [(self.sact, 1)])
                self._stg_guard[j % 2] = self.sact.n
                self._stgj += 1
        wvmark = self.sact.n
        guard = {}
        g0 = self.spe.n
        for qs in range(4):
            xnm = self.regen_xn(qs)
            self.wait("tensor", self.sdve, xnm)
            sl = slice(qs * 512, qs * 512 + 512)
            for c in range(NCH):
                self.v(lambda e, c=c, qs=qs, XN=XN, POOLF=POOLF: e.tensor_reduce(
                    POOLF[:, c, qs * 16:qs * 16 + 16],
                    XN[:, c, :].rearrange("p (l g) -> p l g", l=16),
                    AX.X, OP.add), [(self.sdve, 1)])
            for m in range(NCH):
                wb, wm = self.wload("qkvK", m, (m % 2) * 2048,
                                    guard.get(m % 2, g0))
                self.wait("tensor", self.sact, wm)
                for c in range(NCH):
                    self.t(lambda e, m=m, c=c, wb=wb, XN=XN: e.matmul(
                        self.bA(m % 2), wb[:, 0, c, :], XN[:, c, :],
                        start=(c == 0), stop=(c == NCH - 1)),
                        [(self.spe, 1)] if c == NCH - 1 else ())
                guard[m % 2] = self.spe.n
                self.wait("vector", self.spe)
                self.v(lambda e, m=m, sl=sl, K=K, CB=CB: e.tensor_scalar(
                    K[:, m, sl], self.bA(m % 2), CB[:, 8 + m:9 + m], None,
                    OP.add), [(self.sdve, 1)])
                self.wait("tensor", self.sdve)
            if qs == 0:
                self.wait("tensor", self.sact, wvmark)
            for tl in range(4):
                kt = qs * 4 + tl
                for ec in range(2):
                    pb = (tl * 2 + ec) % 2
                    for c in range(NCH):
                        self.t(lambda e, tl=tl, ec=ec, c=c, pb=pb, XN=XN,
                               WVH=WVH: e.matmul(
                            self.bB(pb), XN[:, c, tl * 128:tl * 128 + 128],
                            WVH[:, c, ec * 512:ec * 512 + 512],
                            start=(c == 0), stop=False))
                    self.t(lambda e, ec=ec, pb=pb: e.matmul(
                        self.bB(pb), self.ONESB[0:1, 0:128],
                        self.CRB.ap()[0:1, ec * 512:ec * 512 + 512],
                        start=False, stop=True), [(self.spe, 1)])
                    self.wait("vector", self.spe)
                    self.v(lambda e, kt=kt, ec=ec, pb=pb, V=V: e.tensor_copy(
                        V[:, kt, ec * 8:ec * 8 + 8, 0:64],
                        self.bB(pb).rearrange("p (h n) -> p h n", h=8)),
                        [(self.sdve, 1)])
                    self.wait("tensor", self.sdve)
        self.wait("vector", self.sdve)
        self.v(lambda e, POOLF=POOLF, POOLB=POOLB: e.tensor_scalar(
            POOLB.rearrange("p c l -> p (c l)"),
            POOLF.rearrange("p c l -> p (c l)"), 1.0 / 32.0, None, OP.mult),
            [(self.sdve, 1)])
        pbm = self.sdve.n
        self.wait("tensor", self.sdve, pbm)
        g0r = self.spe.n
        for m in range(4):
            wb, wm = self.wload("rkvT", m, (m % 2) * 2048, guard.get(m % 2, g0r))
            self.wait("tensor", self.sact, wm)
            for c in range(NCH):
                self.t(lambda e, m=m, c=c, wb=wb, POOLB=POOLB: e.matmul(
                    self.bA(m)[:, 0:64], wb[:, 0, c, :], POOLB[:, c, :],
                    start=(c == 0), stop=(c == NCH - 1)),
                    [(self.spe, 1)] if c == NCH - 1 else ())
            guard[m % 2] = self.spe.n
        self.wait("vector", self.spe)
        for m in range(2):
            self.v(lambda e, m=m, RKF=RKF, CB=CB: e.tensor_scalar(
                RKF[:, m, :], self.bA(m)[:, 0:64], CB[:, 26 + m:27 + m], None,
                OP.add), [(self.sdve, 1)])
        for m in range(2):
            self.v(lambda e, m=m, RVF=RVF: e.tensor_copy(
                RVF[:, m, :], self.bA(2 + m)[:, 0:64]), [(self.sdve, 1)])
        self.wait("tensor", self.sdve)
        for j in range(2):
            self.t(lambda e, j=j, RVF=RVF: e.matmul(
                self.bB(j)[0:64, 0:128], RVF[:, j, :], self.IDT,
                start=True, stop=False))
            self.t(lambda e, j=j: e.matmul(
                self.bB(j)[0:64, 0:128], self.ONESB[0:1, 0:64],
                self.CRB.ap()[0:1, 1024 + j * 128:1024 + j * 128 + 128],
                start=False, stop=True), [(self.spe, 1)])
        self.wait("vector", self.spe)
        for j in range(2):
            self.v(lambda e, j=j, RV=RV: e.tensor_copy(
                RV[0:64, j * 2:j * 2 + 2, 0:64],
                self.bB(j)[0:64, 0:128].rearrange("p (h n) -> p h n", h=2)),
                [(self.sdve, 1)])
        self.wait("tensor", self.sdve)
        self.wait("vector", self.dcc, mlmark)

    def l1_p3(self, seq, qc):
        d = self.d
        K = self.vw(A_K, BF16, [128, NCH, S])
        V = self.vw(A_V, BF16, [128, 16, H0, 65])
        XN = self.vw(A_XN, BF16, [128, NCH, 512])
        Q = self.vw(A_Q, BF16, [128, NCH, 512])
        XQ = self.vw(A_Q, E5, [128, NCH, 512])
        G = self.vw(A_G, BF16, [128, NCH, 512])
        H1 = self.vw(A_U, E5, [128, 32, 512])
        LTM = self.vw(A_U, BF16, [128, 4, 1024])
        LFM = self.vw(A_U + 8192, BF16, [128, NCH, 512])
        PL = self.vw(A_U + 16384, BF16, [128, LH, 512])
        LT = self.vw(A_U + 20480, BF16, [128, 4, 256])
        LF = self.vw(A_U + 22528, BF16, [128, 2, 128])
        MIX = self.vw(A_U, BF16, [128, NCH, 512])
        RQ = self.vw(A_XN, BF16, [128, 2, 512])
        GT = self.vw(A_XN + 2048, BF16, [128, 1024])
        SUBS = self.vw(A_XN + 4096, BF16, [128, NCH, 128])
        TMB = self.vw(A_XN + 6144, BF16, [128, NCH, 128])
        RM = self.vw(T_C, BF16, [128, NCH, 128], arena=self.TMP)
        T1 = self.vw(T_A, BF16, [128, 512], arena=self.TMP)
        T2 = self.vw(T_B, BF16, [128, 512], arena=self.TMP)
        TF = self.vw(T_C, F32, [128, 512], arena=self.TMP)
        LC4 = self.vw(T_D, F32, [128, 16], arena=self.TMP)
        REC4 = self.vw(T_D + 128, F32, [128, 16], arena=self.TMP)
        ML = self.vw(A_CCSS, BF16, [64, S])
        RKF = self.vw(A_CCSS + 7168, BF16, [128, 2, LAT])
        RV = self.vw(A_CCSS + 7424, BF16, [64, LH, 65])
        CB = self.CB.ap()
        B1 = self.vw(A_B1, F32, [128, S])
        B2 = self.vw(A_B2, F32, [128, S])
        xr = self.XR.ap()
        sl = slice(qc * 512, qc * 512 + 512)
        c44 = 44.3614195558

        self.wait("scalar", self.sact, self.cbq_mark)
        self.wait("vector", self.sdve, self.stats_done)
        for c in range(NCH):
            self.v(lambda e, c=c, sl=sl, TF=TF: e.tensor_tensor(
                TF[:], xr[:, c, sl], B2[:, sl], OP.mult))
            self.v(lambda e, c=c, sl=sl, TF=TF: e.tensor_tensor(
                TF[:], TF[:], B1[:, sl], OP.subtract), [(self.sdve, 1)])
            self.wait("scalar", self.sdve)
            self.s(lambda e, c=c, TF=TF, XQ=XQ: e.activation(
                XQ[:, c, :], TF[:], AF.Copy), [(self.sact, 1)])
            self.wait("vector", self.sact)
        gW = {}
        g0f = self.spe.n
        for m in range(32):
            wb, wm = self.wload("w1T", m, (m % 2) * 2048, gW.get(m % 2, g0f),
                                odt=E5)
            self.wait("tensor", self.sact, wm)
            for c in range(NCH):
                self.t(lambda e, m=m, c=c, wb=wb, XQ=XQ: e.matmul(
                    self.bA(m % 2), wb[:, 0, c, :], XQ[:, c, :],
                    start=(c == 0), stop=(c == NCH - 1)),
                    [(self.spe, 1)] if c == NCH - 1 else ())
            gW[m % 2] = self.spe.n
            self.wait("scalar", self.spe)
            if m > 1:
                self.wait("scalar", self.sact)
            self.s(lambda e, m=m, T1=T1, CB=CB: e.activation(
                T1[:], self.bA(m % 2), AF.Relu, bias=CB[:, 46 + m:47 + m]),
                [(self.sact, 1)])
            self.wait("scalar", self.sact)
            self.s(lambda e, m=m, T1=T1, H1=H1: e.activation(
                H1[:, m, :], T1[:], AF.Copy), [(self.sact, 1)])
        h1m = self.sact.n
        self.wait("tensor", self.sact, h1m)
        wd_guard = self.spe.n
        for m in range(32):
            wb, wm = self.wload("w2T", m, (m % 2) * 2048, wd_guard, odt=E5)
            self.wait("tensor", self.sact, wm)
            for dt in range(8):
                bank = self.bA(dt) if dt < 4 else self.bB(dt - 4)
                self.t(lambda e, m=m, dt=dt, bank=bank, wb=wb, H1=H1: e.matmul(
                    bank, wb[:, 0, dt, :], H1[:, m, :],
                    start=(m == 0), stop=(m == 31)),
                    [(self.spe, 1)] if dt == 7 else ())
            wd_guard = self.spe.n
        self.wait("scalar", self.spe)
        self.wait("vector", self.spe)
        dmk = {}
        for dt in range(8):
            o = d["dropm"][seq].rearrange("(c p) t -> p c t", p=128)
            self.wait("sync", self.sdve)
            self.y(lambda e, dt=dt, o=o, T2=T2: e.dma_start(
                T2[:], o[:, dt, sl]), [(self.ddm, 16)])
            dmk[dt] = self.ddm.n
            bank = self.bA(dt) if dt < 4 else self.bB(dt - 4)
            self.wait("vector", self.ddm, dmk[dt])
            self.v(lambda e, dt=dt, bank=bank, T1=T1, CB=CB: e.tensor_scalar(
                T1[:], bank, CB[:, 78 + dt:79 + dt], None, OP.add))
            self.v(lambda e, T1=T1, T2=T2, TF=TF: e.tensor_tensor(
                TF[:], T1[:], T2[:], OP.mult), [(self.sdve, 1)])
            self.wait("scalar", self.sdve)
            self.s(lambda e, TF=TF, T1=T1: e.activation(
                T1[:], TF[:], AF.Tanh, scale=0.5), [(self.sact, 1)])
            self.wait("vector", self.sact)
            self.v(lambda e, T1=T1, T2=T2: e.tensor_scalar(
                T2[:], T1[:], 1.0, 0.5, OP.add, op1=OP.mult))
            self.v(lambda e, dt=dt, G=G, T2=T2, TF=TF: e.tensor_tensor(
                G[:, dt, :], T2[:], TF[:], OP.mult), [(self.sdve, 1)])
        gmark = self.sdve.n
        if _STAGE == "l1a":
            return
        self.wait("tensor", self.sdve, gmark)
        guard = {}
        g0q = self.spe.n
        for m in range(NCH):
            wb, wm = self.wload("qkvQ", m, (m % 2) * 2048, guard.get(m % 2, g0q))
            self.wait("tensor", self.sact, wm)
            for c in range(NCH):
                self.t(lambda e, m=m, c=c, wb=wb, XN=XN: e.matmul(
                    self.bA(m % 2), wb[:, 0, c, :], XN[:, c, :],
                    start=(c == 0), stop=(c == NCH - 1)),
                    [(self.spe, 1)] if c == NCH - 1 else ())
            guard[m % 2] = self.spe.n
            self.wait("vector", self.spe)
            self.v(lambda e, m=m, Q=Q, CB=CB: e.tensor_scalar(
                Q[:, m, :], self.bA(m % 2), CB[:, m:m + 1], None, OP.add),
                [(self.sdve, 1)])
            self.wait("tensor", self.sdve)
        for m in range(2):
            wb, wm = self.wload("rqT", m, (m % 2) * 2048, guard.get(m % 2))
            self.wait("tensor", self.sact, wm)
            for c in range(NCH):
                self.t(lambda e, m=m, c=c, wb=wb, XN=XN: e.matmul(
                    self.bB(2 + m), wb[:, 0, c, :], XN[:, c, :],
                    start=(c == 0), stop=(c == NCH - 1)),
                    [(self.spe, 1)] if c == NCH - 1 else ())
            guard[m % 2] = self.spe.n
        self.wait("vector", self.spe)
        for m in range(2):
            self.v(lambda e, m=m, RQ=RQ, CB=CB: e.tensor_scalar(
                RQ[:, m, :], self.bB(2 + m), CB[:, 24 + m:25 + m], None,
                OP.add), [(self.sdve, 1)])
        rqm = self.sdve.n
        exp_mark = {}
        evac_mark = 0
        for qt in range(4):
            gt = qc * 4 + qt
            for h in range(H0):
                hb, hc = (h % 2) * 64, h // 2
                bank = self.bA(h % 2)
                if h >= 2:
                    self.wait("tensor", self.sdve, exp_mark[h - 2])
                if qt > 0 and h < 2:
                    self.wait("tensor", self.sdve, evac_mark)
                self.t(lambda e, gt=gt, qt=qt, hb=hb, hc=hc, bank=bank, K=K,
                       Q=Q: e.matmul(
                    bank[:, 0:128], K[hb:hb + 64, hc, gt * 128:gt * 128 + 128],
                    Q[hb:hb + 64, hc, qt * 128:qt * 128 + 128],
                    start=True, stop=True))
                if gt > 0:
                    self.t(lambda e, gt=gt, qt=qt, hb=hb, hc=hc, bank=bank,
                           K=K, Q=Q: e.matmul(
                        bank[:, 128:256],
                        K[hb:hb + 64, hc, gt * 128 - 128:gt * 128],
                        Q[hb:hb + 64, hc, qt * 128:qt * 128 + 128],
                        start=True, stop=True), [(self.spe, 1)])
                else:
                    self.t(lambda e: e.nop(), [(self.spe, 1)])
                self.wait("scalar", self.spe)
                PTL = T1 if h % 2 == 0 else T2
                w = 256 if gt > 0 else 128
                self.s(lambda e, bank=bank, PTL=PTL, w=w: e.activation(
                    PTL[:, 0:w], bank[:, 0:w], AF.Exp, bias=c44, scale=0.125),
                    [(self.sact, 1)])
                self.wait("vector", self.sact)
                self.v(lambda e, PTL=PTL: e.tensor_tensor(
                    PTL[:, 0:128], PTL[:, 0:128], self.M0, OP.mult),
                    [(self.sdve, 1)] if gt == 0 else ())
                if gt > 0:
                    MM = self.M1E if gt % 2 == 0 else self.M1
                    self.v(lambda e, PTL=PTL, MM=MM: e.tensor_tensor(
                        PTL[:, 128:256], PTL[:, 128:256], MM, OP.mult),
                        [(self.sdve, 1)])
                exp_mark[h] = self.sdve.n
                self.wait("tensor", self.sdve, exp_mark[h])
                pvb = self.PSB.ap()[:, (h // 8) * 512 + (h % 8) * 64:
                                    (h // 8) * 512 + (h % 8) * 64 + 64]
                self.t(lambda e, gt=gt, h=h, pvb=pvb, PTL=PTL, V=V: e.matmul(
                    pvb, PTL[:, 0:128], V[:, gt, h, 0:64],
                    start=True, stop=(gt == 0), skip_group_check=True))
                if gt > 0:
                    self.t(lambda e, gt=gt, h=h, pvb=pvb, PTL=PTL, V=V:
                           e.matmul(
                        pvb, PTL[:, 128:256], V[:, gt - 1, h, 0:64],
                        start=False, stop=True, skip_group_check=True))
                lcb = self.PSB.ap()[:, 2 * 512 + h:2 * 512 + h + 1]
                self.t(lambda e, lcb=lcb, PTL=PTL: e.matmul(
                    lcb, PTL[:, 0:128], self.ONESB[:, 0:1],
                    start=True, stop=(gt == 0), skip_group_check=True))
                if gt > 0:
                    self.t(lambda e, lcb=lcb, PTL=PTL: e.matmul(
                        lcb, PTL[:, 128:256], self.ONESB[:, 0:1],
                        start=False, stop=True, skip_group_check=True),
                        [(self.spe, 1)])
                else:
                    self.t(lambda e: e.nop(), [(self.spe, 1)])
            self.wait("vector", self.spe)
            self.v(lambda e, LC4=LC4: e.tensor_scalar(
                LC4[:], self.PSB.ap()[:, 2 * 512:2 * 512 + 16], 1e-30, None,
                OP.max), [(self.sdve, 1)])
            self.wait("vector", self.sdve)
            self.v(lambda e, LC4=LC4, REC4=REC4: e.reciprocal(
                REC4[:], LC4[:]), [(self.sdve, 1)])
            self.wait("vector", self.sdve)
            for h in range(H0):
                pvb = self.PSB.ap()[:, (h // 8) * 512 + (h % 8) * 64:
                                    (h // 8) * 512 + (h % 8) * 64 + 64]
                self.v(lambda e, qt=qt, h=h, pvb=pvb, LTM=LTM, REC4=REC4:
                       e.tensor_scalar(
                    LTM[:, qt, h * 64:h * 64 + 64], pvb, REC4[:, h:h + 1],
                    None, OP.mult), [(self.sdve, 1)])
            evac_mark = self.sdve.n
        self.wait("tensor", self.sdve, evac_mark)
        for qt in range(4):
            for eg in range(2):
                for j in range(4):
                    ec = eg * 4 + j
                    po = (2 + eg) * 512 + j * 128
                    self.t(lambda e, qt=qt, ec=ec, po=po, LTM=LTM: e.matmul(
                        self.PSA.ap()[:, po:po + 128],
                        LTM[:, qt, ec * 128:ec * 128 + 128], self.IDT,
                        start=True, stop=True),
                        [(self.spe, 1)] if j == 3 else ())
                self.wait("vector", self.spe)
                self.v(lambda e, qt=qt, eg=eg, LFM=LFM: e.tensor_copy(
                    LFM[:, eg * 4:eg * 4 + 4, qt * 128:qt * 128 + 128],
                    self.bA(2 + eg).rearrange("p (j n) -> p j n", j=4)),
                    [(self.sdve, 1)])
                self.wait("tensor", self.sdve)
        if _STAGE == "l1b":
            return
        self.wait("tensor", self.sdve)
        for lh in range(LH):
            self.t(lambda e, lh=lh, RKF=RKF, RQ=RQ: e.matmul(
                self.bA(lh)[0:64, :],
                RKF[(lh % 2) * 64:(lh % 2) * 64 + 64, lh // 2, :],
                RQ[(lh % 2) * 64:(lh % 2) * 64 + 64, lh // 2, :],
                start=True, stop=True), [(self.spe, 1)])
            self.wait("scalar", self.spe)
            self.s(lambda e, lh=lh, PL=PL: e.activation(
                PL[0:64, lh, :], self.bA(lh)[0:64, :], AF.Exp, bias=c44,
                scale=0.125), [(self.sact, 1)])
            self.wait("vector", self.sact)
            self.v(lambda e, lh=lh, PL=PL, ML=ML, sl=sl: e.tensor_tensor(
                PL[0:64, lh, :], PL[0:64, lh, :], ML[0:64, sl], OP.mult),
                [(self.sdve, 1)])
        plm = self.sdve.n
        self.wait("tensor", self.sdve, plm)
        mixm = 0
        g0t = self.spe.n
        rt_guard = {}
        for t4 in range(4):
            rwb = {}
            for m in range(2):
                rwb[m], rwm = self.wload("routT", m, (m % 2) * 2048,
                                         rt_guard.get(m % 2, g0t))
                self.wait("tensor", self.sact, rwm)
            for lh in range(LH):
                self.t(lambda e, t4=t4, lh=lh, PL=PL, RV=RV: e.matmul(
                    self.bB(3)[:, lh * 65:lh * 65 + 65],
                    PL[0:64, lh, t4 * 128:t4 * 128 + 128], RV[0:64, lh, :],
                    start=True, stop=True, skip_group_check=True),
                    [(self.spe, 1)] if lh == 3 else ())
            self.wait("vector", self.spe)
            for lh in range(LH):
                self.v(lambda e, lh=lh, LC4=LC4: e.tensor_scalar(
                    LC4[:, lh:lh + 1],
                    self.bB(3)[:, lh * 65 + 64:lh * 65 + 65], 1e-30, None,
                    OP.max), [(self.sdve, 1)] if lh == 3 else ())
            self.wait("vector", self.sdve)
            self.v(lambda e, LC4=LC4, REC4=REC4: e.reciprocal(
                REC4[:, 0:4], LC4[:, 0:4]), [(self.sdve, 1)])
            self.wait("vector", self.sdve)
            for lh in range(LH):
                self.v(lambda e, t4=t4, lh=lh, LT=LT, REC4=REC4:
                       e.tensor_scalar(
                    LT[:, t4, lh * 64:lh * 64 + 64],
                    self.bB(3)[:, lh * 65:lh * 65 + 64], REC4[:, lh:lh + 1],
                    None, OP.mult), [(self.sdve, 1)])
            self.wait("tensor", self.sdve)
            for j in range(2):
                self.t(lambda e, t4=t4, j=j, LT=LT: e.matmul(
                    self.bA(0 if j == 0 else 0)[:, j * 128:j * 128 + 128]
                    if False else
                    self.PSA.ap()[:, j * 128:j * 128 + 128],
                    LT[:, t4, j * 128:j * 128 + 128], self.IDT,
                    start=True, stop=True), [(self.spe, 1)] if j == 1 else ())
            self.wait("vector", self.spe)
            self.v(lambda e, LF=LF: e.tensor_copy(
                LF.rearrange("p c n -> p (c n)"), self.PSA.ap()[:, 0:256]),
                [(self.sdve, 1)])
            self.wait("tensor", self.sdve)
            for dc in range(8):
                pb = self.PSA.ap()[:, 512 + dc * 128 if dc < 4 else
                                   1024 + (dc - 4) * 128]
                pb = self.PSA.ap()[:, (1 + dc // 4) * 512 + (dc % 4) * 128:
                                   (1 + dc // 4) * 512 + (dc % 4) * 128 + 128]
                for c2 in range(2):
                    self.t(lambda e, dc=dc, c2=c2, pb=pb, w=rwb[c2], LF=LF:
                           e.matmul(
                        pb, w[:, 0, dc, :], LF[:, c2, :],
                        start=(c2 == 0), stop=(c2 == 1),
                        skip_group_check=True),
                        [(self.spe, 1)] if (dc == 7 and c2 == 1) else ())
            rt_guard[0] = rt_guard[1] = self.spe.n
            self.wait("vector", self.spe)
            for dc in range(8):
                pb = self.PSA.ap()[:, (1 + dc // 4) * 512 + (dc % 4) * 128:
                                   (1 + dc // 4) * 512 + (dc % 4) * 128 + 128]
                self.v(lambda e, dc=dc, pb=pb, RM=RM, CB=CB: e.tensor_scalar(
                    RM[:, dc, :], pb, CB[:, 30 + dc:31 + dc], None, OP.add),
                    [(self.sdve, 1)])
            rmm = self.sdve.n
            self.wait("tensor", self.sdve, rmm)
            for hc in range(NCH):
                gb = self.PSB.ap()[:, (hc // 4) * 512 + (hc % 4) * 128:
                                   (hc // 4) * 512 + (hc % 4) * 128 + 128]
                self.t(lambda e, hc=hc, t4=t4, gb=gb, LFM=LFM: e.matmul(
                    gb[0:2, :], self.GLGR.ap()[:, 0:2],
                    LFM[:, hc, t4 * 128:t4 * 128 + 128],
                    start=True, stop=False, skip_group_check=True))
                self.t(lambda e, hc=hc, gb=gb, RM=RM: e.matmul(
                    gb[0:2, :], self.GLGR.ap()[:, 2:4], RM[:, hc, :],
                    start=False, stop=True, skip_group_check=True),
                    [(self.spe, 1)] if hc == NCH - 1 else ())
            self.wait("scalar", self.spe)
            for half in range(2):
                self.s(lambda e, half=half, GT=GT: e.activation(
                    GT[0:2, half * 512:half * 512 + 512],
                    self.PSB.ap()[0:2, half * 512:half * 512 + 512],
                    AF.Tanh, scale=0.5), [(self.sact, 1)])
            self.wait("vector", self.sact)
            self.v(lambda e, GT=GT: e.tensor_scalar(
                GT[0:2, :], GT[0:2, :], 1.0, 0.5, OP.add, op1=OP.mult),
                [(self.sdve, 1)])
            g2m = self.sdve.n
            for hc in range(NCH):
                self.v(lambda e, hc=hc, t4=t4, SUBS=SUBS, LFM=LFM, RM=RM:
                       e.tensor_tensor(
                    SUBS[:, hc, :], LFM[:, hc, t4 * 128:t4 * 128 + 128],
                    RM[:, hc, :], OP.subtract),
                    [(self.sdve, 1)] if hc == NCH - 1 else ())
            self.wait("tensor", self.sdve)
            for hc in range(NCH):
                gbp = self.PSA.ap()[:, 384:512]
                self.t(lambda e, hc=hc, gbp=gbp, GT=GT: e.matmul(
                    gbp, self.E2.ap(), GT[0:2, hc * 128:hc * 128 + 128],
                    start=True, stop=True), [(self.spe, 1)])
                self.wait("vector", self.spe)
                self.v(lambda e, hc=hc, gbp=gbp, TMB=TMB, SUBS=SUBS:
                       e.tensor_tensor(
                    TMB[:, hc, :], SUBS[:, hc, :], gbp, OP.mult),
                    [(self.sdve, 1)])
                self.wait("tensor", self.sdve)
                self.v(lambda e, hc=hc, t4=t4, MIX=MIX, TMB=TMB, RM=RM:
                       e.tensor_tensor(
                    MIX[:, hc, t4 * 128:t4 * 128 + 128], TMB[:, hc, :],
                    RM[:, hc, :], OP.add), [(self.sdve, 1)])
            mixm = self.sdve.n
        if _STAGE == "l1c":
            return
        self.wait("tensor", self.sdve, mixm)
        guard = {}
        g0o = self.spe.n
        for m in range(NCH):
            wb, wm = self.wload("outT", m, (m % 2) * 2048, guard.get(m % 2, g0o))
            self.wait("tensor", self.sact, wm)
            for ec in range(NCH):
                self.t(lambda e, m=m, ec=ec, wb=wb, MIX=MIX: e.matmul(
                    self.bA(m % 2), wb[:, 0, ec, :], MIX[:, ec, :],
                    start=(ec == 0), stop=(ec == NCH - 1)),
                    [(self.spe, 1)] if ec == NCH - 1 else ())
            guard[m % 2] = self.spe.n
            self.wait("vector", self.spe)
            self.v(lambda e, m=m, TF=TF, CB=CB: e.tensor_scalar(
                TF[:], self.bA(m % 2), CB[:, 38 + m:39 + m], None, OP.add))
            self.v(lambda e, m=m, TF=TF, G=G: e.tensor_tensor(
                TF[:], TF[:], G[:, m, :], OP.mult))
            self.v(lambda e, m=m, sl=sl, TF=TF: e.tensor_tensor(
                self.XR.ap()[:, m, sl], self.XR.ap()[:, m, sl], TF[:],
                OP.add), [(self.sdve, 1)])
            self.wait("tensor", self.sdve)

    def dbg_store_xn(self, seq):
        B1 = self.vw(A_B1, F32, [128, S])
        B2 = self.vw(A_B2, F32, [128, S])
        XNF = self.vw(A_XN, F32, [128, 512])
        TMPF = self.vw(T_C, F32, [128, 512], arena=self.TMP)
        xr = self.XR.ap()
        o = self.d["out"][seq].rearrange("(c p) t -> p c t", p=128)
        self.wait("vector", self.sdve, self.stats_done)
        dbgout = globals().get("_DBGOUT", "xn")
        for qs in range(4):
            sl = slice(qs * 512, qs * 512 + 512)
            for c in range(NCH):
                if dbgout == "b2":
                    self.v(lambda e, c=c, sl=sl, XNF=XNF: e.tensor_copy(
                        XNF[:], B2[:, sl]), [(self.sdve, 1)])
                elif dbgout == "b1":
                    self.v(lambda e, c=c, sl=sl, XNF=XNF: e.tensor_copy(
                        XNF[:], B1[:, sl]), [(self.sdve, 1)])
                elif dbgout == "x":
                    self.v(lambda e, c=c, sl=sl, XNF=XNF: e.tensor_copy(
                        XNF[:], xr[:, c, sl]), [(self.sdve, 1)])
                else:
                    self.v(lambda e, c=c, sl=sl, TMPF=TMPF: e.tensor_tensor(
                        TMPF[:], xr[:, c, sl], B2[:, sl], OP.mult))
                    self.v(lambda e, c=c, sl=sl, XNF=XNF, TMPF=TMPF: e.tensor_tensor(
                        XNF[:], TMPF[:], B1[:, sl], OP.subtract), [(self.sdve, 1)])
                self.wait("sync", self.sdve)
                self.y(lambda e, c=c, sl=sl, o=o, XNF=XNF: e.dma_start(
                    o[:, c, sl], XNF[:]), [(self.dmas, 16)])
                self.wait("vector", self.dmas)
        self.wait("sync", self.dmas)

    def store_out(self, seq):
        o = self.d["out"][seq].rearrange("(c p) t -> p c t", p=128)
        self.wait("sync", self.sdve)
        self.y(lambda e, o=o: e.dma_start(o, self.XR.ap()), [(self.dmas, 16)])


def _build():
    nc = bass.Bass("TRN2", target_bir_lowering=False,
                   detect_race_conditions=False, dynamic_dma_scratch_size=1024)
    st = ExitStack()
    with st:
        kb = KB(nc, st)
        with nc.Block() as block:
            @block.tensor
            def _(eng):
                kb.cur, kb.e = "tensor", eng
                kb.program()

            @block.vector
            def _(eng):
                kb.cur, kb.e = "vector", eng
                kb.program()

            @block.scalar
            def _(eng):
                kb.cur, kb.e = "scalar", eng
                kb.program()

            @block.sync
            def _(eng):
                kb.cur, kb.e = "sync", eng
                kb.program()

            @block.gpsimd
            def _(eng):
                kb.cur, kb.e = "gpsimd", eng
                kb.program()
    return nc


def kernel(**inputs):
    in_maps = _host_prep(inputs)
    nc = _build()
    res = run_bass_kernel_spmd(nc, in_maps, core_ids=list(range(NCORES)))
    outs = [r["out"] for r in res.results]
    full = np.concatenate(outs, axis=0)
    return np.ascontiguousarray(full.transpose(0, 2, 1)).astype(np.float32)
